# version 01 (final)
# baseline (speedup 1.0000x reference)
import numpy as np
from contextlib import ExitStack

import concourse.bass as bass
import concourse.mybir as mybir
from concourse.bass_utils import run_bass_kernel_spmd

F32 = mybir.dt.float32
BF16 = mybir.dt.bfloat16
I32 = mybir.dt.int32
U16 = mybir.dt.uint16
AF = mybir.ActivationFunctionType
ALU = mybir.AluOpType
AX = mybir.AxisListType

NCORES = 8
D = 2048
TC = 2048
TB = 512
NTB = TC // TB
HC = D // 128
NFF = 16384
SEQ = 8192

PERM = np.concatenate([np.arange(0, D, 2), np.arange(1, D, 2)])
PERM_N = np.concatenate([np.arange(0, NFF, 2), np.arange(1, NFF, 2)])
IPERM = np.argsort(PERM)

NEG = -1.0e30
EPS = 1e-5
ISQ = float(1.0 / np.sqrt(128.0))


class Prog:

    def __init__(self, nc):
        self.nc = nc
        self.ops = []
        self.counts = {}

    def on(self, eng, fn):
        self.ops.append((eng, fn))

    def inc(self, sem, n=1):
        self.counts[sem] = self.counts.get(sem, 0) + n
        return self.counts[sem]

    def val(self, sem):
        return self.counts.get(sem, 0)

    def wait(self, eng, sem, val):
        if val > 0:
            self.on(eng, lambda e, s=sem, v=val: e.wait_ge(s, v))

    def dma(self, eng, out, in_, sem, **kw):
        tgt = self.inc(sem, 16)
        self.on(eng, lambda e, o=out, i=in_, s=sem, kw=kw: e.dma_start(out=o, in_=i, **kw).then_inc(s, 16))
        return tgt

    def dmaT(self, eng, out, in_, sem):
        tgt = self.inc(sem, 16)
        self.on(eng, lambda e, o=out, i=in_, s=sem: e.dma_start_transpose(o, i).then_inc(s, 16))
        return tgt

    def play(self, block):
        def make(engname):
            def body(eng):
                for en, fn in self.ops:
                    if en == engname:
                        fn(eng)
            return body
        block.sync(make("sync"))
        block.gpsimd(make("gpsimd"))
        block.vector(make("vector"))
        block.scalar(make("scalar"))
        block.tensor(make("tensor"))


ENG2NAME = {"sync": "sync", "gpsimd": "gpsimd", "vector": "vector", "scalar": "scalar", "tensor": "tensor"}


def _prep_shared(inp):
    C = np.ascontiguousarray
    out = {}
    out["gate_pT"] = C(inp["L0_ffn_gate_W_packed"][PERM_N].T)
    out["up_pT"] = C(inp["L0_ffn_up_W_packed"][PERM_N].T)
    out["gate_sT"] = C(inp["L0_ffn_gate_scales"][PERM_N].T)
    out["up_sT"] = C(inp["L0_ffn_up_scales"][PERM_N].T)
    out["down_pT"] = C(inp["L0_ffn_down_W_packed"][PERM].T)
    out["down_sT"] = C(inp["L0_ffn_down_scales"][PERM].T)

    dperm = np.concatenate([np.arange(0, 128, 2), np.arange(1, 128, 2)])
    rp = np.arange(6144)
    for h in range(16):
        rp[h * 128:(h + 1) * 128] = h * 128 + dperm
        rp[2048 + h * 128:2048 + (h + 1) * 128] = 2048 + h * 128 + dperm
    out["w_attenT"] = C(inp["L0_attn_w_atten"][rp][:, PERM].T)
    out["w_projT"] = C(inp["L0_attn_w_proj"][PERM].T)

    out["WqT"] = C(inp["L1_attn_Wq"][PERM])
    out["WkT"] = C(inp["L1_attn_Wk"][PERM])
    out["WvT"] = C(inp["L1_attn_Wv"][PERM])
    out["WoT"] = C(inp["L1_attn_Wo"][:, PERM])
    out["bo"] = C(inp["L1_attn_bo"][PERM].reshape(HC, 128).T)

    out["gate_w"] = C(inp["L1_ffn_gate_w"][PERM])
    out["ebias"] = C(inp["L1_ffn_expert_bias"].reshape(1, 16))
    out["e_gu"] = C(inp["L1_ffn_experts_gate_up"][:, PERM, :])
    out["e_dn"] = C(inp["L1_ffn_experts_down"][:, :, PERM])
    out["sh_g"] = C(inp["L1_ffn_shared_gate_w"][PERM])
    out["sh_u"] = C(inp["L1_ffn_shared_up_w"][PERM])
    out["sh_d"] = C(inp["L1_ffn_shared_down_w"][:, PERM])
    return out


def _rope_tables(t0):
    theta = 1.0 / (10000.0 ** (np.arange(0, 128, 2, dtype=np.float64) / 128.0))
    pos = np.arange(t0, t0 + TC, dtype=np.float64)
    ang = np.outer(theta, pos)
    return np.cos(ang).astype(np.float32), np.sin(ang).astype(np.float32)


def _x_shard(x, c):
    b, t0 = divmod(c, 4)
    sh = x[b, t0 * TC:(t0 + 1) * TC, :]
    return np.ascontiguousarray(sh[:, PERM].T.astype(np.float32))


def _x_unshard(shards):
    out = np.empty((2, SEQ, D), np.float32)
    for c, sh in enumerate(shards):
        b, t0 = divmod(c, 4)
        out[b, t0 * TC:(t0 + 1) * TC, :] = sh.T[:, IPERM]
    return out


def emit_ln(p, sems, W, xres, xln, xsq, ones_f, ones_r, stat, bc, banks, xwait):
    s_sq, s_pe, s_dve, s_act = sems["sq"], sems["pe"], sems["dve"], sems["act"]
    s_ss = sems["ss"]
    pb_sum, pb_ssq, pb_bc0, pb_bc1 = banks[6], banks[7], banks[4], banks[5]
    mean = stat[:, 0 * W:1 * W]
    var = stat[:, 1 * W:2 * W]
    rstd = stat[:, 2 * W:3 * W]
    nmr = stat[:, 3 * W:4 * W]
    xr = lambda hc: xres[:, hc * W:(hc + 1) * W]
    xl = lambda hc: xln[:, hc * W:(hc + 1) * W]

    base_sq = p.val(s_sq)
    base_ss = p.val(s_ss)
    p.wait("tensor", s_dve, p.val(s_dve))
    p.wait("scalar", s_dve, p.val(s_dve))
    p.wait("scalar", xwait[0], xwait[1])
    for hc in range(HC):
        if hc >= 4:
            p.wait("scalar", s_ss, base_ss + hc - 3)
        p.inc(s_sq)
        p.on("scalar", lambda e, hc=hc: e.activation(
            xsq[hc % 4][:], xr(hc), AF.Square).then_inc(s_sq, 1))
    p.wait("tensor", xwait[0], xwait[1])
    for hc in range(HC):
        p.on("tensor", lambda e, hc=hc: e.matmul(
            pb_sum[0:1, :], ones_f, xr(hc), start=(hc == 0), stop=(hc == HC - 1)))
        p.wait("tensor", s_sq, base_sq + hc + 1)
        p.inc(s_ss)
        p.on("tensor", lambda e, hc=hc: e.matmul(
            pb_ssq[0:1, :], ones_f, xsq[hc % 4][:],
            start=(hc == 0), stop=(hc == HC - 1)).then_inc(s_ss, 1))

    p.wait("vector", s_ss, base_ss + HC)
    d1 = p.inc(s_dve)
    p.on("vector", lambda e: e.tensor_single_scalar(mean, pb_sum[0:1, :], 1.0 / D,
                                                    ALU.mult).then_inc(s_dve, 1))
    p.wait("scalar", s_dve, d1)
    a1 = p.inc(s_act)
    p.on("scalar", lambda e: e.activation(nmr, mean, AF.Square).then_inc(s_act, 1))
    p.wait("vector", s_act, a1)
    p.on("vector", lambda e: e.tensor_scalar(
        var, pb_ssq[0:1, :], 1.0 / D, EPS, ALU.mult, ALU.add))
    d2 = p.inc(s_dve)
    p.on("vector", lambda e: e.tensor_tensor(var, var, nmr, ALU.subtract).then_inc(s_dve, 1))
    p.wait("scalar", s_dve, d2)
    a2 = p.inc(s_act)
    p.on("scalar", lambda e: e.activation(rstd, var, AF.Sqrt).then_inc(s_act, 1))
    p.wait("vector", s_act, a2)
    dr = p.inc(s_dve)
    p.on("vector", lambda e: e.reciprocal(rstd, rstd).then_inc(s_dve, 1))
    p.wait("vector", s_dve, dr)
    d3 = p.inc(s_dve)
    p.on("vector", lambda e: e.scalar_tensor_tensor(
        nmr, mean, -1.0, rstd, ALU.mult, ALU.mult).then_inc(s_dve, 1))

    p.wait("tensor", s_dve, d3)
    p.on("tensor", lambda e: e.matmul(pb_bc0[:], ones_r, rstd, start=True, stop=True))
    v2 = p.inc(s_pe)
    p.on("tensor", lambda e: e.matmul(pb_bc1[:], ones_r, nmr, start=True, stop=True).then_inc(s_pe, 1))
    p.wait("scalar", s_pe, v2)
    p.on("scalar", lambda e: e.copy(bc[0][:], pb_bc0[:]))
    a3 = p.inc(s_act)
    p.on("scalar", lambda e: e.copy(bc[1][:], pb_bc1[:]).then_inc(s_act, 1))

    p.wait("vector", s_act, a3)
    d_ln = None
    for hc in range(HC):
        p.on("vector", lambda e, hc=hc: e.tensor_tensor(xsq[hc % 4][:], xr(hc), bc[0][:], ALU.mult))
        last = hc == HC - 1
        if last:
            d_ln = p.inc(s_dve)
        p.on("vector", lambda e, hc=hc, last=last: (
            e.tensor_tensor(xl(hc), xsq[hc % 4][:], bc[1][:], ALU.add).then_inc(s_dve, 1)
            if last else
            e.tensor_tensor(xl(hc), xsq[hc % 4][:], bc[1][:], ALU.add)))
    return d_ln


def build_ffn_program(p, nc, ctx, xin, xout, wg_lo, wg_hi, wu_lo, wu_hi, wd_lo, wd_hi,
                      tag="f", scale=0.5, phases=3):
    sb = lambda name, shape, dt: ctx.enter_context(nc.sbuf_tensor(tag + name, shape, dt))
    ps = lambda name, shape, dt: ctx.enter_context(nc.psum_tensor(tag + name, shape, dt))
    sem = lambda name: ctx.enter_context(nc.semaphore(tag + name))

    xres = sb("xres", [128, HC * TB], F32)
    xln = sb("xln", [128, HC * TB], BF16)
    xsq = [sb(f"xsq{i}", [128, TB], F32) for i in range(4)]
    stat = sb("stat", [1, 4 * TB], F32)
    bc = [sb(f"bc{i}", [128, TB], F32) for i in range(2)]
    ones_f = sb("ones_f", [128, 1], F32)
    ones_r = sb("ones_r", [1, 128], F32)
    inter = sb("inter", [128, 64 * TB], BF16)
    gel = [sb(f"gel{i}", [128, TB], BF16) for i in range(2)]
    NSL = 256
    JPS = NSL // 128
    wgbuf = [sb(f"wg{i}", [128, 16 * NSL], BF16) for i in range(2)]
    wubuf = [sb(f"wu{i}", [128, 16 * NSL], BF16) for i in range(2)]
    wdbuf = [sb(f"wd{i}", [128, 8 * 1024], BF16) for i in range(2)]
    banks = [ps(f"pb{i}", [128, TB], F32) for i in range(8)]

    sems = {k: sem("_" + k) for k in
            ("x", "w", "w2", "sq", "ss", "pe", "dve", "act", "gu", "gel", "ep",
             "dj", "ev", "st", "init")}

    p.on("gpsimd", lambda e: e.memset(ones_f[:], 1.0))
    ii = p.inc(sems["init"])
    p.on("gpsimd", lambda e: e.memset(ones_r[:], 1.0).then_inc(sems["init"], 1))
    p.wait("tensor", sems["init"], ii)

    xin_t = xin.rearrange("(hc p) t -> p hc t", p=128)
    xout_t = xout.rearrange("(hc p) t -> p hc t", p=128)
    wsrc = {}
    for nm, ap in (("gl", wg_lo), ("gh", wg_hi), ("ul", wu_lo), ("uh", wu_hi)):
        wsrc[nm] = ap.rearrange("(kc p) n -> p kc n", p=128)
    wdl = wd_lo.rearrange("(j p) h -> p j h", p=128)
    wdh = wd_hi.rearrange("(j p) h -> p j h", p=128)

    for tb in range(NTB):
        t0 = tb * TB
        if tb > 0:
            p.wait("gpsimd", sems["st"], 16 * tb)
        xv = p.dma("gpsimd", xres[:].rearrange("p (hc t) -> p hc t", t=TB),
                   xin_t[:, :, t0:t0 + TB], sems["x"])

        p.wait("tensor", sems["ev"], p.val(sems["ev"]))
        d_ln = emit_ln(p, sems, TB, xres[:], xln[:], xsq, ones_f[:], ones_r[:], stat[:], bc,
                       banks, (sems["x"], xv))

        for nh in range(2 if phases >= 2 else 0):
            nbase = nh * 8192
            gel_base = p.val(sems["gel"])
            ep_base = p.val(sems["ep"])
            wv_list = []
            NSLABS = 8192 // NSL
            for s in range(NSLABS):
                gs = p.val(sems["gu"]) // JPS + s
                if gs >= 2:
                    p.wait("sync", sems["gu"], (gs - 1) * JPS)
                par = s % 2
                n0 = nbase + s * NSL
                wv = None
                for nm, buf, off in (("gl", wgbuf[par], 0), ("gh", wgbuf[par], 8 * NSL),
                                     ("ul", wubuf[par], 0), ("uh", wubuf[par], 8 * NSL)):
                    wv = p.dma("sync", buf[:, off:off + 8 * NSL].rearrange("p (kc n) -> p kc n", n=NSL),
                               wsrc[nm][:, :, n0:n0 + NSL], sems["w"])
                wv_list.append(wv)

            p.wait("tensor", sems["dve"], d_ln)
            p.wait("tensor", sems["ev"], p.val(sems["ev"]))
            for j in range(64):
                s = j // JPS
                jj = j % JPS
                par = s % 2
                jpar = j % 2
                if jj == 0:
                    p.wait("tensor", sems["w"], wv_list[s])
                if j >= 2:
                    p.wait("tensor", sems["gel"], gel_base + j - 1)
                    p.wait("tensor", sems["ep"], ep_base + j - 1)
                bG, bU = banks[jpar], banks[2 + jpar]
                for kc in range(16):
                    woff = (kc % 8) * NSL + jj * 128 + (8 * NSL if kc >= 8 else 0)
                    xlv = xln[:, kc * TB:(kc + 1) * TB]
                    p.on("tensor", lambda e, par=par, woff=woff, xlv=xlv, bG=bG, kc=kc:
                         e.matmul(bG[:], wgbuf[par][:, woff:woff + 128], xlv,
                                  start=(kc == 0), stop=(kc == 15)))
                gv = None
                for kc in range(16):
                    woff = (kc % 8) * NSL + jj * 128 + (8 * NSL if kc >= 8 else 0)
                    xlv = xln[:, kc * TB:(kc + 1) * TB]
                    last = kc == 15
                    if last:
                        gv = p.inc(sems["gu"])
                    p.on("tensor", lambda e, par=par, woff=woff, xlv=xlv, bU=bU, kc=kc, last=last:
                         (e.matmul(bU[:], wubuf[par][:, woff:woff + 128], xlv,
                                   start=(kc == 0), stop=(kc == 15)).then_inc(sems["gu"], 1)
                          if last else
                          e.matmul(bU[:], wubuf[par][:, woff:woff + 128], xlv,
                                   start=(kc == 0), stop=(kc == 15))))
                p.wait("scalar", sems["gu"], gv)
                if j >= 2:
                    p.wait("scalar", sems["ep"], ep_base + j - 1)
                g_ = p.inc(sems["gel"])
                p.on("scalar", lambda e, jpar=jpar, bG=bG: e.activation(
                    gel[jpar][:], bG[:], AF.Gelu_apprx_tanh).then_inc(sems["gel"], 1))
                p.wait("vector", sems["gel"], g_)
                e_ = p.inc(sems["ep"])
                p.on("vector", lambda e, j=j, jpar=jpar, bU=bU: e.tensor_tensor(
                    inter[:, j * TB:(j + 1) * TB], gel[jpar][:], bU[:],
                    ALU.mult).then_inc(sems["ep"], 1))

            ep_all = p.val(sems["ep"])
            wsrc_d = wdl if nh == 0 else wdh
            for hh in range(2 if phases >= 21 or phases == 3 else 0):
                w2_base = p.val(sems["w2"])
                for s in range(8):
                    ds = p.val(sems["dj"]) // 8 + s
                    if ds >= 2:
                        p.wait("sync", sems["dj"], (ds - 1) * 8)
                    p.dma("sync", wdbuf[s % 2][:].rearrange("p (j h) -> p j h", h=1024),
                          wsrc_d[:, s * 8:(s + 1) * 8, hh * 1024:hh * 1024 + 1024], sems["w2"])
                p.wait("tensor", sems["ep"], ep_all)
                p.wait("tensor", sems["ev"], p.val(sems["ev"]))
                for j in range(64 if phases >= 22 or phases == 3 else 0):
                    s = j // 8
                    if j % 8 == 0:
                        p.wait("tensor", sems["w2"], w2_base + 16 * (s + 1))
                    for h8 in range(8):
                        woff = (j % 8) * 1024 + h8 * 128
                        jdone = h8 == 7
                        if jdone:
                            p.inc(sems["dj"])
                        p.on("tensor", lambda e, s=s, woff=woff, j=j, h8=h8, jdone=jdone:
                             (e.matmul(banks[h8][:], wdbuf[s % 2][:, woff:woff + 128],
                                       inter[:, j * TB:(j + 1) * TB],
                                       start=(j == 0), stop=(j == 63)).then_inc(sems["dj"], 1)
                              if jdone else
                              e.matmul(banks[h8][:], wdbuf[s % 2][:, woff:woff + 128],
                                       inter[:, j * TB:(j + 1) * TB],
                                       start=(j == 0), stop=(j == 63))))
                p.wait("vector", sems["dj"], p.val(sems["dj"]))
                for h8 in range(8 if phases >= 23 or phases == 3 else 0):
                    hc = hh * 8 + h8
                    last = h8 == 7
                    if last:
                        p.inc(sems["ev"])
                    p.on("vector", lambda e, h8=h8, hc=hc, last=last:
                         (e.scalar_tensor_tensor(
                             xres[:, hc * TB:(hc + 1) * TB], banks[h8][:], scale,
                             xres[:, hc * TB:(hc + 1) * TB], ALU.mult, ALU.add).then_inc(sems["ev"], 1)
                          if last else
                          e.scalar_tensor_tensor(
                              xres[:, hc * TB:(hc + 1) * TB], banks[h8][:], scale,
                              xres[:, hc * TB:(hc + 1) * TB], ALU.mult, ALU.add)))

        p.wait("gpsimd", sems["ev"], p.val(sems["ev"]))
        p.dma("gpsimd", xout_t[:, :, t0:t0 + TB],
              xres[:].rearrange("p (hc t) -> p hc t", t=TB), sems["st"])
    p.wait("gpsimd", sems["st"], p.val(sems["st"]))
    return sems


def build_ffn_nc(phases=3):
    nc = bass.Bass("TRN2", target_bir_lowering=False, debug=False)
    xin = nc.dram_tensor("xin", [D, TC], F32, kind="ExternalInput").ap()
    wg_lo = nc.dram_tensor("wg_lo", [NFF // 16, NFF], BF16, kind="ExternalInput").ap()
    wg_hi = nc.dram_tensor("wg_hi", [NFF // 16, NFF], BF16, kind="ExternalInput").ap()
    wu_lo = nc.dram_tensor("wu_lo", [NFF // 16, NFF], BF16, kind="ExternalInput").ap()
    wu_hi = nc.dram_tensor("wu_hi", [NFF // 16, NFF], BF16, kind="ExternalInput").ap()
    wd_lo = nc.dram_tensor("wd_lo", [NFF // 2, D], BF16, kind="ExternalInput").ap()
    wd_hi = nc.dram_tensor("wd_hi", [NFF // 2, D], BF16, kind="ExternalInput").ap()
    xout = nc.dram_tensor("xout", [D, TC], F32, kind="ExternalOutput").ap()
    with ExitStack() as ctx:
        p = Prog(nc)
        build_ffn_program(p, nc, ctx, xin, xout, wg_lo, wg_hi, wu_lo, wu_hi, wd_lo, wd_hi, phases=phases)
        block = ctx.enter_context(nc.Block())
        p.play(block)
    return nc


def emit_dequant(p, nc, ctx, jobs, tag="q"):
    sb = lambda name, shape, dt: ctx.enter_context(nc.sbuf_tensor(tag + name, shape, dt))
    sem = lambda name: ctx.enter_context(nc.semaphore(tag + name))
    W = max(j[0].shape[1] for j in jobs)
    wp_s = [sb(f"wp{i}", [128, W], I32) for i in range(2)]
    sc_s = [sb(f"sc{i}", [128, W], BF16) for i in range(2)]
    w16 = sb("w16", [128, W], U16)
    nib = sb("nib", [128, W], U16)
    pat = sb("pat", [128, W], U16)
    pat2 = sb("pat2", [128, W], U16)
    sgn = sb("sgn", [128, W], U16)
    mag = sb("mag", [128, W], BF16)
    olo = [sb(f"olo{i}", [128, W], BF16) for i in range(2)]
    ohi = [sb(f"ohi{i}", [128, W], BF16) for i in range(2)]
    s_ld = sem("_ld")
    s_dv = sem("_dv")
    s_st = sem("_st")

    for t, (wp, se, out_lo, out_hi) in enumerate(jobs):
        n = wp.shape[1]
        par = t % 2
        if t >= 2:
            p.wait("gpsimd", s_dv, t - 1)
        lv = p.dma("gpsimd", wp_s[par][:, 0:n], wp, s_ld)
        lv = p.dma("gpsimd", sc_s[par][:, 0:n], se, s_ld)
        p.wait("vector", s_ld, lv)
        if t >= 2:
            p.wait("vector", s_st, 32 * (t - 1))
        def chain(e, par=par, n=n, t=t):
            e.tensor_copy(w16[:, 0:n], wp_s[par][:, 0:n])
            for hi in (0, 1):
                outt = (ohi if hi else olo)[par]
                if hi:
                    e.tensor_scalar(nib[:, 0:n], w16[:, 0:n], 4, 15,
                                    ALU.logical_shift_right, ALU.bitwise_and)
                else:
                    e.tensor_single_scalar(nib[:, 0:n], w16[:, 0:n], 15, ALU.bitwise_and)
                e.tensor_scalar(pat[:, 0:n], nib[:, 0:n], 7, 6, ALU.bitwise_and, ALU.logical_shift_left)
                e.tensor_single_scalar(pat2[:, 0:n], pat[:, 0:n], 16256, ALU.add)
                e.tensor_scalar(sgn[:, 0:n], nib[:, 0:n], 3, 15,
                                ALU.logical_shift_right, ALU.logical_shift_left)
                raw = pat2[:, 0:n].bitcast(BF16)
                e.tensor_single_scalar(mag[:, 0:n], raw, 0.5, ALU.mult)
                e.scalar_tensor_tensor(mag[:, 0:n], raw, -1.0, mag[:, 0:n], ALU.add, ALU.min)
                e.tensor_tensor(outt[:, 0:n].bitcast(U16), mag[:, 0:n].bitcast(U16),
                                sgn[:, 0:n], ALU.bitwise_xor)
                ins = e.tensor_tensor(outt[:, 0:n], outt[:, 0:n], sc_s[par][:, 0:n], ALU.mult)
                if hi:
                    ins.then_inc(s_dv, 1)
        p.inc(s_dv)
        p.on("vector", chain)
        p.wait("sync", s_dv, t + 1)
        p.dma("sync", out_lo, olo[par][:, 0:n], s_st)
        p.dma("sync", out_hi, ohi[par][:, 0:n], s_st)
    p.wait("sync", s_st, p.val(s_st))


def build_dequant_nc():
    nc = bass.Bass("TRN2", target_bir_lowering=False, debug=False)
    gp = nc.dram_tensor("gp", [1024, 2048], I32, kind="ExternalInput").ap()
    gs = nc.dram_tensor("gs", [1024, 2048], F32, kind="ExternalInput").ap()
    up = nc.dram_tensor("up", [1024, 2048], I32, kind="ExternalInput").ap()
    us = nc.dram_tensor("us", [1024, 2048], F32, kind="ExternalInput").ap()
    dp = nc.dram_tensor("dp", [1024, 2048], I32, kind="ExternalInput").ap()
    ds = nc.dram_tensor("ds", [1024, 2048], F32, kind="ExternalInput").ap()
    outs = {}
    for nm in ("glo", "ghi", "ulo", "uhi", "dlo", "dhi"):
        outs[nm] = nc.dram_tensor(nm, [1024, 2048], BF16, kind="ExternalOutput").ap()
    jobs = []
    for i in range(8):
        r = slice(128 * i, 128 * (i + 1))
        jobs.append((gp[r, :], gs[r, :], outs["glo"][r, :], outs["ghi"][r, :]))
        jobs.append((up[r, :], us[r, :], outs["ulo"][r, :], outs["uhi"][r, :]))
        jobs.append((dp[r, :], ds[r, :], outs["dlo"][r, :], outs["dhi"][r, :]))
    with ExitStack() as ctx:
        p = Prog(nc)
        emit_dequant(p, nc, ctx, jobs)
        block = ctx.enter_context(nc.Block())
        p.play(block)
    return nc


def dequant_shard_inputs(pre, c):
    C = np.ascontiguousarray
    gsl = slice(2048 * c, 2048 * (c + 1))
    dsl = slice(1024 * c, 1024 * (c + 1))
    rep = lambda a: np.repeat(a, 16, axis=0)
    return dict(
        gp=C(pre["gate_pT"][:, gsl]), gs=C(rep(pre["gate_sT"][:, gsl])),
        up=C(pre["up_pT"][:, gsl]), us=C(rep(pre["up_sT"][:, gsl])),
        dp=C(pre["down_pT"][dsl, :]), ds=C(rep(pre["down_sT"][64 * c:64 * (c + 1), :])),
    )


def dequant_assemble(results):
    wg_lo = np.concatenate([r["glo"] for r in results], axis=1)
    wg_hi = np.concatenate([r["ghi"] for r in results], axis=1)
    wu_lo = np.concatenate([r["ulo"] for r in results], axis=1)
    wu_hi = np.concatenate([r["uhi"] for r in results], axis=1)
    wd_lo = np.concatenate([r["dlo"] for r in results], axis=0)
    wd_hi = np.concatenate([r["dhi"] for r in results], axis=0)
    return wg_lo, wg_hi, wu_lo, wu_hi, wd_lo, wd_hi


def build_qkv0_program(p, nc, ctx, xin, watT, cosd, sind, q_fm, k_fm, v_tm, tag="a"):
    sb = lambda name, shape, dt: ctx.enter_context(nc.sbuf_tensor(tag + name, shape, dt))
    ps = lambda name, shape, dt: ctx.enter_context(nc.psum_tensor(tag + name, shape, dt))
    sem = lambda name: ctx.enter_context(nc.semaphore(tag + name))

    xres = sb("xres", [128, HC * TB], F32)
    xln = sb("xln", [128, HC * TB], BF16)
    xsq = [sb(f"xsq{i}", [128, TB], F32) for i in range(4)]
    stat = sb("stat", [1, 4 * TB], F32)
    bc = [sb(f"bc{i}", [128, TB], F32) for i in range(2)]
    ones_f = sb("ones_f", [128, 1], F32)
    ones_r = sb("ones_r", [1, 128], F32)
    ctile = sb("cos", [64, TC], F32)
    stile = sb("sin", [64, TC], F32)
    rt = [sb(f"rt{i}", [64, TB], F32) for i in range(2)]
    qkbuf = sb("qkbuf", [128, 32 * TB], BF16)
    vobuf = sb("vobuf", [128, 4 * 2048], BF16)
    wqk = [sb(f"wqk{i}", [128, 16 * 128], BF16) for i in range(2)]
    wv = [sb(f"wv{i}", [128, 16 * 512], BF16) for i in range(2)]
    banks = [ps(f"pb{i}", [128, TB], F32) for i in range(8)]

    sems = {k: sem("_" + k) for k in
            ("x", "w", "wv", "sq", "ss", "pe", "dve", "act", "qk", "rp", "vv",
             "ve", "st", "init")}

    p.on("gpsimd", lambda e: e.memset(ones_f[:], 1.0))
    ii = p.inc(sems["init"])
    p.on("gpsimd", lambda e: e.memset(ones_r[:], 1.0).then_inc(sems["init"], 1))
    p.wait("tensor", sems["init"], ii)
    p.dma("gpsimd", ctile[:], cosd, sems["x"])
    p.dma("gpsimd", stile[:], sind, sems["x"])

    xin_t = xin.rearrange("(hc p) t -> p hc t", p=128)
    wat = watT.rearrange("(kc p) m -> p kc m", p=128)
    qf = q_fm.rearrange("(mc p) t -> p mc t", p=128)
    kf = k_fm.rearrange("(mc p) t -> p mc t", p=128)
    vt = v_tm.rearrange("(tc p) v -> p tc v", p=128)

    for tb in range(NTB):
        t0 = tb * TB
        if tb > 0:
            p.wait("gpsimd", sems["st"], 48 * tb)
        xv = p.dma("gpsimd", xres[:].rearrange("p (hc t) -> p hc t", t=TB),
                   xin_t[:, :, t0:t0 + TB], sems["x"])
        d_ln = emit_ln(p, sems, TB, xres[:], xln[:], xsq, ones_f[:], ones_r[:],
                       stat[:], bc, banks, (sems["x"], xv))

        for mc in range(32):
            par = mc % 2
            gmc = p.val(sems["qk"])
            if gmc >= 2:
                p.wait("gpsimd", sems["qk"], gmc - 1)
            wvv = p.dma("gpsimd", wqk[par][:].rearrange("p (kc m) -> p kc m", m=128),
                        wat[:, :, mc * 128:(mc + 1) * 128], sems["w"])
            p.wait("tensor", sems["w"], wvv)
            if mc == 0:
                p.wait("tensor", sems["dve"], d_ln)
            if gmc >= 2:
                p.wait("tensor", sems["rp"], gmc - 1)
            qv = None
            for kc in range(16):
                last = kc == 15
                if last:
                    qv = p.inc(sems["qk"])
                p.on("tensor", lambda e, par=par, kc=kc, last=last:
                     (e.matmul(banks[par][:], wqk[par][:, kc * 128:(kc + 1) * 128],
                               xln[:, kc * TB:(kc + 1) * TB],
                               start=(kc == 0), stop=last).then_inc(sems["qk"], 1)
                      if last else
                      e.matmul(banks[par][:], wqk[par][:, kc * 128:(kc + 1) * 128],
                               xln[:, kc * TB:(kc + 1) * TB], start=(kc == 0), stop=last)))
            p.wait("vector", sems["qk"], qv)
            pb = banks[par]
            out = qkbuf[:, mc * TB:(mc + 1) * TB]
            cs = ctile[:, t0:t0 + TB]
            sn = stile[:, t0:t0 + TB]
            p.on("vector", lambda e, pb=pb, cs=cs: e.tensor_tensor(rt[0][:], pb[0:64, :], cs, ALU.mult))
            p.on("vector", lambda e, pb=pb, sn=sn: e.tensor_tensor(rt[1][:], pb[64:128, :], sn, ALU.mult))
            p.on("vector", lambda e, out=out: e.tensor_tensor(out[0:64, :], rt[0][:], rt[1][:], ALU.subtract))
            p.on("vector", lambda e, pb=pb, cs=cs: e.tensor_tensor(rt[0][:], pb[64:128, :], cs, ALU.mult))
            p.on("vector", lambda e, pb=pb, sn=sn: e.tensor_tensor(rt[1][:], pb[0:64, :], sn, ALU.mult))
            rpv = p.inc(sems["rp"])
            p.on("vector", lambda e, out=out: e.tensor_tensor(
                out[64:128, :], rt[0][:], rt[1][:], ALU.add).then_inc(sems["rp"], 1))

        for vs in range(4):
            par = vs % 2
            gvs = p.val(sems["vv"]) // 4
            if gvs >= 2:
                p.wait("gpsimd", sems["vv"], (gvs - 1) * 4)
            wvv = p.dma("gpsimd", wv[par][:].rearrange("p (kc v) -> p kc v", v=512),
                        wat[:, :, 4096 + vs * 512:4096 + (vs + 1) * 512], sems["wv"])
            p.wait("tensor", sems["wv"], wvv)
            if vs == 0:
                p.wait("tensor", sems["act"], p.val(sems["act"]))
                p.wait("tensor", sems["ve"], p.val(sems["ve"]))
            for tc in range(4):
                gve = p.val(sems["ve"])
                if gve >= 4:
                    p.wait("tensor", sems["ve"], gve - 3 + tc)
                vvv = None
                for kc in range(16):
                    last = kc == 15
                    if last:
                        vvv = p.inc(sems["vv"])
                    p.on("tensor", lambda e, par=par, tc=tc, kc=kc, last=last:
                         (e.matmul(banks[4 + tc][:], xln[:, kc * TB + tc * 128:kc * TB + (tc + 1) * 128],
                                   wv[par][:, kc * 512:(kc + 1) * 512],
                                   start=(kc == 0), stop=last).then_inc(sems["vv"], 1)
                          if last else
                          e.matmul(banks[4 + tc][:], xln[:, kc * TB + tc * 128:kc * TB + (tc + 1) * 128],
                                   wv[par][:, kc * 512:(kc + 1) * 512], start=(kc == 0), stop=last)))
                p.wait("scalar", sems["vv"], vvv)
                vev = p.inc(sems["ve"])
                p.on("scalar", lambda e, tc=tc, vs=vs: e.copy(
                    vobuf[:, tc * 2048 + vs * 512:tc * 2048 + (vs + 1) * 512],
                    banks[4 + tc][:]).then_inc(sems["ve"], 1))

        p.wait("gpsimd", sems["rp"], p.val(sems["rp"]))
        p.dma("gpsimd", qf[:, :, t0:t0 + TB],
              qkbuf[:, 0:16 * TB].rearrange("p (mc t) -> p mc t", t=TB), sems["st"])
        p.dma("gpsimd", kf[:, :, t0:t0 + TB],
              qkbuf[:, 16 * TB:32 * TB].rearrange("p (mc t) -> p mc t", t=TB), sems["st"])
        p.wait("gpsimd", sems["ve"], p.val(sems["ve"]))
        p.dma("gpsimd", vt[:, 4 * tb:4 * (tb + 1), :],
              vobuf[:].rearrange("p (tc v) -> p tc v", v=2048), sems["st"])
    p.wait("gpsimd", sems["st"], p.val(sems["st"]))
    return sems


def build_qkv0_nc():
    nc = bass.Bass("TRN2", target_bir_lowering=False, debug=False)
    xin = nc.dram_tensor("xin", [D, TC], F32, kind="ExternalInput").ap()
    watT = nc.dram_tensor("watT", [D, 6144], F32, kind="ExternalInput").ap()
    cosd = nc.dram_tensor("cosd", [64, TC], F32, kind="ExternalInput").ap()
    sind = nc.dram_tensor("sind", [64, TC], F32, kind="ExternalInput").ap()
    q_fm = nc.dram_tensor("q_fm", [D, TC], BF16, kind="ExternalOutput").ap()
    k_fm = nc.dram_tensor("k_fm", [D, TC], BF16, kind="ExternalOutput").ap()
    v_tm = nc.dram_tensor("v_tm", [TC, D], BF16, kind="ExternalOutput").ap()
    with ExitStack() as ctx:
        p = Prog(nc)
        build_qkv0_program(p, nc, ctx, xin, watT, cosd, sind, q_fm, k_fm, v_tm)
        block = ctx.enter_context(nc.Block())
        p.play(block)
    return nc


KW0 = 7680 + 4 * TB

def build_attn0_program(p, nc, ctx, xin, q_fm, k_loc, v_loc, wprojT, vb_d, xout,
                        nheads=16, kvh=None, tag="b"):
    sb = lambda name, shape, dt: ctx.enter_context(nc.sbuf_tensor(tag + name, shape, dt))
    ps = lambda name, shape, dt: ctx.enter_context(nc.psum_tensor(tag + name, shape, dt))
    sem = lambda name: ctx.enter_context(nc.semaphore(tag + name))

    qt = sb("qt", [128, 16 * TB], BF16)
    kbuf = [sb(f"kb{i}", [128, 16 * TB], BF16) for i in range(2)]
    vbuf = [sb(f"vb{i}", [128, 4 * 2048], BF16) for i in range(2)]
    oacc = sb("oacc", [128, 16 * TB], F32)
    sacc = sb("sacc", [1, 16 * TB], F32)
    ybuf = sb("ybuf", [128, 16 * TB], BF16)
    pt = [sb(f"pt{i}", [128, TB], BF16) for i in range(2)]
    msk = [sb(f"msk{i}", [128, TB], F32) for i in range(4)]
    wp = [sb(f"wp{i}", [128, 16 * 128], BF16) for i in range(2)]
    vb = sb("vbt", [128, 64], F32)
    ones_b = sb("ones_b", [128, 1], BF16)
    ones_r = sb("ones_r", [1, 128], F32)
    banks = [ps(f"pb{i}", [128, TB], F32) for i in range(8)]

    sems = {k: sem("_" + k) for k in
            ("x", "k", "w", "st", "s1", "msk", "exp", "pv", "oev", "ybc", "yy",
             "pj", "ev", "init")}

    for i in range(4):
        p.on("gpsimd", lambda e, i=i: e.memset(msk[i][:], 0.0))
        p.on("gpsimd", lambda e, i=i: e.affine_select(
            out=msk[i][:], in_=msk[i][:], compare_op=ALU.is_ge, fill=NEG,
            base=-i * 128, pattern=[[1, TB]], channel_multiplier=-1))
    p.on("gpsimd", lambda e: e.memset(ones_b[:], 1.0))
    ii = p.inc(sems["init"])
    p.on("gpsimd", lambda e: e.memset(ones_r[:], 1.0).then_inc(sems["init"], 1))
    p.wait("tensor", sems["init"], ii)
    p.wait("scalar", sems["init"], ii)
    p.wait("vector", sems["init"], ii)
    p.dma("gpsimd", vb[:], vb_d, sems["x"])

    xin_t = xin.rearrange("(hc p) t -> p hc t", p=128)
    xout_t = xout.rearrange("(hc p) t -> p hc t", p=128)
    qf = q_fm.rearrange("(mc p) t -> p mc t", p=128)
    kl = k_loc.rearrange("(mc p) t -> p mc t", p=128)
    vl = v_loc.rearrange("(kc p) v -> p kc v", p=128)
    wpj = wprojT.rearrange("(ic p) o -> p ic o", p=128)

    NMS = 16
    for qb in range(NTB):
        t0 = qb * TB
        if qb > 0:
            p.wait("gpsimd", sems["st"], 16 * qb)
            p.wait("vector", sems["st"], 16 * qb)
        qv = p.dma("gpsimd", qt[:].rearrange("p (h t) -> p h t", t=TB),
                   qf[:, :, t0:t0 + TB], sems["x"])
        for ms in range(NMS):
            par = ms % 2
            kcol = ms * TB if ms < 15 else 7680 + qb * TB
            gms = p.val(sems["k"]) // 32
            if gms >= 2:
                p.wait("gpsimd", sems["pv"], (gms - 1) * 64)
            kv = p.dma("gpsimd", kbuf[par][:].rearrange("p (h t) -> p h t", t=TB),
                       kl[:, :, kcol:kcol + TB], sems["k"])
            kv = p.dma("gpsimd", vbuf[par][:].rearrange("p (kc v) -> p kc v", v=2048),
                       vl[:, kcol // 128:kcol // 128 + 4, :], sems["k"])
            jcol = qb * 16 + ms
            for h in range(nheads):
                kh = h if kvh is None else h // (nheads // kvh)
                for kc in range(4):
                    spar = kc % 2
                    gkc = p.val(sems["s1"])
                    if h == 0 and kc == 0:
                        p.wait("tensor", sems["k"], kv)
                        if qb == 0 and ms == 0:
                            p.wait("tensor", sems["x"], qv)
                    if gkc >= 2:
                        p.wait("tensor", sems["exp"], gkc - 1)
                    sv = p.inc(sems["s1"])
                    p.on("tensor", lambda e, par=par, spar=spar, kh=kh, kc=kc, h=h:
                         e.matmul(banks[spar][:], kbuf[par][:, kh * TB + kc * 128:kh * TB + (kc + 1) * 128],
                                  qt[:, h * TB:(h + 1) * TB], start=True, stop=True).then_inc(sems["s1"], 1))
                    diag = ms == 15
                    if diag:
                        p.wait("vector", sems["s1"], sv)
                        mv = p.inc(sems["msk"])
                        p.on("vector", lambda e, spar=spar, kc=kc: e.tensor_tensor(
                            banks[spar][:], banks[spar][:], msk[kc][:], ALU.add).then_inc(sems["msk"], 1))
                        p.wait("scalar", sems["msk"], mv)
                    else:
                        p.wait("scalar", sems["s1"], sv)
                    gpt = p.val(sems["exp"])
                    if gpt >= 2:
                        p.wait("scalar", sems["pv"], gpt - 1)
                    ppar = gpt % 2
                    xv2 = p.inc(sems["exp"])
                    p.on("scalar", lambda e, spar=spar, ppar=ppar, jcol=jcol: e.activation(
                        pt[ppar][:], banks[spar][:], AF.Exp, bias=vb[:, jcol:jcol + 1],
                        scale=ISQ).then_inc(sems["exp"], 1))
                    p.wait("tensor", sems["exp"], xv2)
                    p.on("tensor", lambda e, ppar=ppar, kc=kc: e.matmul(
                        banks[3][0:1, :], ones_b[:], pt[ppar][:], start=(kc == 0), stop=(kc == 3)))
                    pvv = p.inc(sems["pv"])
                    p.on("tensor", lambda e, ppar=ppar, par=par, h=h, kc=kc: e.matmul(
                        banks[2][:], vbuf[par][:, kc * 2048 + h * 128:kc * 2048 + (h + 1) * 128],
                        pt[ppar][:], start=(kc == 0), stop=(kc == 3)).then_inc(sems["pv"], 1))
                p.wait("vector", sems["pv"], p.val(sems["pv"]))
                first = ms == 0
                ov = p.inc(sems["oev"])
                if first:
                    p.on("vector", lambda e, h=h: e.tensor_copy(oacc[:, h * TB:(h + 1) * TB], banks[2][:]))
                    p.on("vector", lambda e, h=h: e.tensor_copy(
                        sacc[:, h * TB:(h + 1) * TB], banks[3][0:1, :]).then_inc(sems["oev"], 1))
                else:
                    p.on("vector", lambda e, h=h: e.tensor_tensor(
                        oacc[:, h * TB:(h + 1) * TB], banks[2][:], oacc[:, h * TB:(h + 1) * TB], ALU.add))
                    p.on("vector", lambda e, h=h: e.tensor_tensor(
                        sacc[:, h * TB:(h + 1) * TB], banks[3][0:1, :],
                        sacc[:, h * TB:(h + 1) * TB], ALU.add).then_inc(sems["oev"], 1))
                p.wait("tensor", sems["oev"], p.val(sems["oev"]))

        for h in range(nheads):
            dv = p.inc(sems["ybc"])
            p.on("vector", lambda e, h=h: e.reciprocal(
                sacc[:, h * TB:(h + 1) * TB], sacc[:, h * TB:(h + 1) * TB]).then_inc(sems["ybc"], 1))
            p.wait("tensor", sems["ybc"], dv)
            if h > 0:
                p.wait("tensor", sems["yy"], p.val(sems["yy"]))
            bv = p.inc(sems["ybc"])
            p.on("tensor", lambda e, h=h: e.matmul(
                banks[4][:], ones_r[:], sacc[:, h * TB:(h + 1) * TB],
                start=True, stop=True).then_inc(sems["ybc"], 1))
            p.wait("vector", sems["ybc"], bv)
            yv = p.inc(sems["yy"])
            p.on("vector", lambda e, h=h: e.tensor_tensor(
                ybuf[:, h * TB:(h + 1) * TB], oacc[:, h * TB:(h + 1) * TB],
                banks[4][:], ALU.mult).then_inc(sems["yy"], 1))

        p.wait("gpsimd", sems["yy"], p.val(sems["yy"]))
        xres = oacc
        xv = p.dma("gpsimd", xres[:].rearrange("p (hc t) -> p hc t", t=TB),
                   xin_t[:, :, t0:t0 + TB], sems["x"])
        y_all = p.val(sems["yy"])
        for oc in range(HC):
            par = oc % 2
            goc = p.val(sems["pj"])
            if goc >= 2:
                p.wait("gpsimd", sems["pj"], goc - 1)
            wv = p.dma("gpsimd", wp[par][:].rearrange("p (ic o) -> p ic o", o=128),
                       wpj[:, :, oc * 128:(oc + 1) * 128], sems["w"])
            p.wait("tensor", sems["w"], wv)
            if oc == 0:
                p.wait("tensor", sems["yy"], y_all)
            if goc >= 2:
                p.wait("tensor", sems["ev"], goc - 1)
            elif oc == 0:
                p.wait("tensor", sems["ev"], p.val(sems["ev"]))
            pjv = None
            for ic in range(HC):
                last = ic == HC - 1
                if last:
                    pjv = p.inc(sems["pj"])
                p.on("tensor", lambda e, par=par, ic=ic, oc=oc, last=last:
                     (e.matmul(banks[5 + oc % 2][:], wp[par][:, ic * 128:(ic + 1) * 128],
                               ybuf[:, ic * TB:(ic + 1) * TB],
                               start=(ic == 0), stop=last).then_inc(sems["pj"], 1)
                      if last else
                      e.matmul(banks[5 + oc % 2][:], wp[par][:, ic * 128:(ic + 1) * 128],
                               ybuf[:, ic * TB:(ic + 1) * TB], start=(ic == 0), stop=last)))
            p.wait("vector", sems["pj"], pjv)
            if oc == 0:
                p.wait("vector", sems["x"], xv)
            evv = p.inc(sems["ev"])
            p.on("vector", lambda e, oc=oc: e.tensor_tensor(
                xres[:, oc * TB:(oc + 1) * TB], banks[5 + oc % 2][:],
                xres[:, oc * TB:(oc + 1) * TB], ALU.add).then_inc(sems["ev"], 1))
        p.wait("gpsimd", sems["ev"], p.val(sems["ev"]))
        p.dma("gpsimd", xout_t[:, :, t0:t0 + TB],
              xres[:].rearrange("p (hc t) -> p hc t", t=TB), sems["st"])
    p.wait("gpsimd", sems["st"], p.val(sems["st"]))
    return sems


def build_attn0_nc():
    nc = bass.Bass("TRN2", target_bir_lowering=False, debug=False)
    xin = nc.dram_tensor("xin", [D, TC], F32, kind="ExternalInput").ap()
    q_fm = nc.dram_tensor("q_fm", [D, TC], BF16, kind="ExternalInput").ap()
    k_loc = nc.dram_tensor("k_loc", [D, KW0], BF16, kind="ExternalInput").ap()
    v_loc = nc.dram_tensor("v_loc", [KW0, D], BF16, kind="ExternalInput").ap()
    wprojT = nc.dram_tensor("wprojT", [D, D], F32, kind="ExternalInput").ap()
    vb_d = nc.dram_tensor("vb_d", [128, 64], F32, kind="ExternalInput").ap()
    xout = nc.dram_tensor("xout", [D, TC], F32, kind="ExternalOutput").ap()
    with ExitStack() as ctx:
        p = Prog(nc)
        build_attn0_program(p, nc, ctx, xin, q_fm, k_loc, v_loc, wprojT, vb_d, xout)
        block = ctx.enter_context(nc.Block())
        p.play(block)
    return nc


def attn0_host_kv(k_shards, v_shards, c):
    cp = c % 4
    grp = range((c // 4) * 4, (c // 4) * 4 + 4)
    k_seq = np.concatenate([k_shards[g] for g in grp], axis=1)
    v_seq = np.concatenate([v_shards[g] for g in grp], axis=0)
    kend = (cp + 1) * TC
    k_loc = np.zeros((D, KW0), k_seq.dtype)
    v_loc = np.zeros((KW0, D), v_seq.dtype)
    k_loc[:, 0:kend - TB] = k_seq[:, 0:kend - TB]
    v_loc[0:kend - TB, :] = v_seq[0:kend - TB, :]
    for qb in range(4):
        kq = cp * TC + (qb + 1) * TB
        k_loc[:, 7680 + qb * TB:7680 + (qb + 1) * TB] = k_seq[:, kq - TB:kq]
        v_loc[7680 + qb * TB:7680 + (qb + 1) * TB, :] = v_seq[kq - TB:kq, :]
    return np.ascontiguousarray(k_loc), np.ascontiguousarray(v_loc)


def attn0_host_vb(c):
    cp = c % 4
    vb = np.zeros((128, 64), np.float32)
    for qb in range(4):
        nreal = 4 * cp + qb
        for s in range(15):
            if s >= nreal:
                vb[:, qb * 16 + s] = NEG
    return vb


MTB = 1024
def build_moe_program(p, nc, ctx, xin, gate_w, ebias, e_gu, e_dn, sh_g, sh_u, sh_d,
                      comb_d, xout, tag="m", scod_d=None, xln_d=None, int_d=None, bce_d=None):
    sb = lambda name, shape, dt: ctx.enter_context(nc.sbuf_tensor(tag + name, shape, dt))
    ps = lambda name, shape, dt: ctx.enter_context(nc.psum_tensor(tag + name, shape, dt))
    sem = lambda name: ctx.enter_context(nc.semaphore(tag + name))

    xres = sb("xres", [128, 2 * HC * TB], F32)
    xln = sb("xln", [128, 2 * HC * TB], BF16)
    xsq = [sb(f"xsq{i}", [128, TB], F32) for i in range(4)]
    stat = sb("stat", [1, 4 * TB], F32)
    bcl = [sb(f"bcl{i}", [128, TB], F32) for i in range(2)]
    ones_f = sb("ones_f", [128, 1], F32)
    ones_r = sb("ones_r", [1, 128], F32)
    idf = sb("idf", [128, 128], F32)
    inter = sb("inter", [128, 8 * MTB], BF16)
    sil = [sb(f"sil{i}", [128, TB], BF16) for i in range(2)]
    wgu = [sb(f"wgu{i}", [128, 2 * 16 * 128], BF16) for i in range(2)]
    wdn = sb("wdn", [128, 8 * 2048], BF16)
    gw = sb("gw", [128, 16 * 16], BF16)
    ebs = sb("ebs", [1, 16], F32)
    ebc = sb("ebc", [128, 16], F32)
    scod = sb("scod", [128, 8 * 16], F32)
    comb = sb("comb", [128, 8 * 16], F32)
    scr = sb("scr", [128, 16], F32)
    bmt = sb("bmt", [128, 16], F32)
    mskt = sb("mskt", [128, 16], I32)
    negt = sb("negt", [128, 16], F32)
    s1t = sb("s1t", [128, 4], F32)
    combf = sb("combf", [16, MTB], F32)
    bce = [sb(f"bce{i}", [128, MTB], F32) for i in range(2)]
    banks = [ps(f"pb{i}", [128, TB], F32) for i in range(8)]

    sems = {k: sem("_" + k) for k in
            ("x", "w", "wd", "bc", "sq", "ss", "pe", "dve", "act", "rt", "sg",
             "rd", "tp", "cf", "cd", "gu", "sl", "ep", "dm", "ev", "st", "init")}

    p.on("gpsimd", lambda e: e.memset(ones_f[:], 1.0))
    p.on("gpsimd", lambda e: e.memset(ones_r[:], 1.0))
    p.on("gpsimd", lambda e: e.memset(negt[:], NEG))
    p.on("gpsimd", lambda e: e.memset(idf[:], 1.0))
    ii = p.inc(sems["init"])
    p.on("gpsimd", lambda e: e.affine_select(
        out=idf[:], in_=idf[:], compare_op=ALU.is_equal, fill=0.0,
        base=0, pattern=[[-1, 128]], channel_multiplier=1).then_inc(sems["init"], 1))
    for eng in ("tensor", "scalar", "vector"):
        p.wait(eng, sems["init"], ii)
    gwv = p.dma("gpsimd", gw[:].rearrange("p (kc n) -> p kc n", n=16),
                gate_w.rearrange("(kc p) n -> p kc n", p=128), sems["x"])
    ebv = p.dma("gpsimd", ebs[:], ebias, sems["x"])
    p.wait("tensor", sems["x"], ebv)
    bv = p.inc(sems["pe"])
    p.on("tensor", lambda e: e.matmul(banks[0][:, 0:16], ones_r[:], ebs[:],
                                      start=True, stop=True).then_inc(sems["pe"], 1))
    p.wait("scalar", sems["pe"], bv)
    bv2 = p.inc(sems["act"])
    p.on("scalar", lambda e: e.copy(ebc[:], banks[0][:, 0:16]).then_inc(sems["act"], 1))

    xin_t = xin.rearrange("(hc p) t -> p hc t", p=128)
    xout_t = xout.rearrange("(hc p) t -> p hc t", p=128)
    gu_src = [e_gu[e].rearrange("(kc p) n -> p kc n", p=128) for e in range(16)]
    dn_src = [e_dn[e].rearrange("(fc p) h -> p fc h", p=128) for e in range(16)]
    shg_src = sh_g.rearrange("(kc p) n -> p kc n", p=128)
    shu_src = sh_u.rearrange("(kc p) n -> p kc n", p=128)
    shd_src = sh_d.rearrange("(fc p) h -> p fc h", p=128)

    for tb in range(TC // MTB):
        t0 = tb * MTB
        if tb > 0:
            p.wait("gpsimd", sems["st"], 32 * tb)
        d_lns = []
        for ts in range(2):
            xv = p.dma("gpsimd",
                       xres[:, ts * HC * TB:(ts + 1) * HC * TB].rearrange("p (hc t) -> p hc t", t=TB),
                       xin_t[:, :, t0 + ts * TB:t0 + (ts + 1) * TB], sems["x"])
            p.wait("tensor", sems["ev"], p.val(sems["ev"]))
            d_ln = emit_ln(p, sems, TB, xres[:, ts * HC * TB:(ts + 1) * HC * TB],
                           xln[:, ts * HC * TB:(ts + 1) * HC * TB], xsq,
                           ones_f[:], ones_r[:], stat[:], bcl, banks, (sems["x"], xv))
            d_lns.append(d_ln)

        p.wait("tensor", sems["dve"], d_lns[1])
        p.wait("tensor", sems["act"], bv2)
        p.wait("tensor", sems["x"], gwv)
        sg_base = p.val(sems["sg"])
        cf_base = p.val(sems["cf"])
        for tc in range(8):
            ts = tc // 4
            tcl = tc % 4
            if tc >= 2:
                p.wait("tensor", sems["sg"], sg_base + tc - 1)
            rtv = None
            for kc in range(16):
                last = kc == 15
                if last:
                    rtv = p.inc(sems["rt"])
                xoff = ts * HC * TB + kc * TB + tcl * 128
                p.on("tensor", lambda e, tc=tc, kc=kc, xoff=xoff, last=last:
                     (e.matmul(banks[tc % 2][:, 0:16], xln[:, xoff:xoff + 128],
                               gw[:, kc * 16:(kc + 1) * 16],
                               start=(kc == 0), stop=last).then_inc(sems["rt"], 1)
                      if last else
                      e.matmul(banks[tc % 2][:, 0:16], xln[:, xoff:xoff + 128],
                               gw[:, kc * 16:(kc + 1) * 16], start=(kc == 0), stop=last)))
            p.wait("scalar", sems["rt"], rtv)
            sgv = p.inc(sems["sg"])
            p.on("scalar", lambda e, tc=tc: e.activation(
                scod[:, tc * 16:(tc + 1) * 16], banks[tc % 2][:, 0:16],
                AF.Sigmoid).then_inc(sems["sg"], 1))
            p.wait("vector", sems["sg"], sgv)
            def RS(fn):
                nv = p.inc(sems["rd"])
                p.on("vector", lambda e, fn=fn, nv=nv: fn(e).then_inc(sems["rd"], 1))
                p.wait("vector", sems["rd"], nv)
            RS(lambda e, tc=tc: e.tensor_tensor(scr[:], scod[:, tc * 16:(tc + 1) * 16], ebc[:], ALU.add))
            RS(lambda e: e.reduce_max(s1t[:, 0:1], scr[:], axis=AX.X))
            RS(lambda e: e.tensor_tensor(mskt[:], scr[:], s1t[:, 0:1].broadcast_to([128, 16]), ALU.is_lt))
            RS(lambda e: e.select(bmt[:], mskt[:], scr[:], negt[:]))
            RS(lambda e: e.reduce_max(s1t[:, 1:2], bmt[:], axis=AX.X))
            RS(lambda e: e.tensor_tensor(scr[:], scr[:], s1t[:, 1:2].broadcast_to([128, 16]), ALU.is_ge))
            RS(lambda e, tc=tc: e.tensor_tensor(scr[:], scr[:], scod[:, tc * 16:(tc + 1) * 16], ALU.mult))
            RS(lambda e: e.reduce_sum(s1t[:, 2:3], scr[:], axis=AX.X))
            RS(lambda e: e.tensor_single_scalar(s1t[:, 2:3], s1t[:, 2:3], 1e-20, ALU.add))
            RS(lambda e: e.reciprocal(s1t[:, 3:4], s1t[:, 2:3]))
            rdv = p.inc(sems["rd"])
            p.on("vector", lambda e, tc=tc: e.tensor_tensor(
                comb[:, tc * 16:(tc + 1) * 16], scr[:], s1t[:, 3:4].broadcast_to([128, 16]),
                ALU.mult).then_inc(sems["rd"], 1))
            p.wait("tensor", sems["rd"], rdv)
            if tc >= 1:
                p.wait("tensor", sems["cf"], cf_base + tc)
            tpv = p.inc(sems["tp"])
            p.on("tensor", lambda e, tc=tc: e.transpose(
                banks[2][0:16, 0:128], comb[:, tc * 16:(tc + 1) * 16],
                idf[:]).then_inc(sems["tp"], 1))
            p.wait("scalar", sems["tp"], tpv)
            cfv = p.inc(sems["cf"])
            p.on("scalar", lambda e, tc=tc: e.copy(
                combf[:, tc * 128:(tc + 1) * 128], banks[2][0:16, 0:128]).then_inc(sems["cf"], 1))
        cfall = p.val(sems["cf"])
        p.wait("gpsimd", sems["cf"], cfall)
        cdv = p.dma("gpsimd", comb_d[:, t0:t0 + MTB], combf[:], sems["cd"])
        if scod_d is not None:
            p.dma("gpsimd", scod_d[:, tb * 128:tb * 128 + 128], scod[:], sems["cd"])
        if xln_d is not None and tb == 0:
            p.dma("gpsimd", xln_d, xln[:], sems["cd"])

        for e_i in range(17):
            shared = e_i == 16
            if not shared:
                bpar = e_i % 2
                if e_i >= 2:
                    p.wait("gpsimd", sems["ep"], p.val(sems["ep"]) - 16)
                p.wait("gpsimd", sems["cd"], cdv)
                bcv = p.dma("gpsimd", bce[bpar][:],
                            comb_d[e_i:e_i + 1, t0:t0 + MTB].broadcast_to([128, MTB]), sems["bc"])
            for j in range(8):
                par = j % 2
                gj = p.val(sems["gu"]) // 2
                if gj >= 2:
                    p.wait("gpsimd", sems["ep"], p.val(sems["ep"]) - 2)
                if shared:
                    wv = p.dma("gpsimd", wgu[par][:, 0:2048].rearrange("p (kc n) -> p kc n", n=128),
                               shg_src[:, :, j * 128:(j + 1) * 128], sems["w"])
                    wv = p.dma("gpsimd", wgu[par][:, 2048:4096].rearrange("p (kc n) -> p kc n", n=128),
                               shu_src[:, :, j * 128:(j + 1) * 128], sems["w"])
                else:
                    wv = p.dma("gpsimd", wgu[par][:, 0:2048].rearrange("p (kc n) -> p kc n", n=128),
                               gu_src[e_i][:, :, j * 128:(j + 1) * 128], sems["w"])
                    wv = p.dma("gpsimd", wgu[par][:, 2048:4096].rearrange("p (kc n) -> p kc n", n=128),
                               gu_src[e_i][:, :, 1024 + j * 128:1024 + (j + 1) * 128], sems["w"])
                for ts in range(2):
                    par2 = (j * 2 + ts) % 2
                    bG, bU = banks[par2], banks[2 + par2]
                    guv = None
                    if ts == 0:
                        p.wait("tensor", sems["w"], wv)
                        if e_i == 0 and j == 0:
                            p.wait("tensor", sems["cf"], cfall)
                        if e_i > 0 and j == 0:
                            p.wait("tensor", sems["dm"], p.val(sems["dm"]))
                    gep = p.val(sems["ep"])
                    if gep >= 2:
                        p.wait("tensor", sems["ep"], gep - 1)
                    for kc in range(16):
                        last = kc == 15
                        if last:
                            guv = p.inc(sems["gu"])
                        xoff = ts * HC * TB + kc * TB
                        p.on("tensor", lambda e, par=par, par2=par2, kc=kc, xoff=xoff, bG=bG, last=last:
                             (e.matmul(bG[:], wgu[par][:, kc * 128:(kc + 1) * 128],
                                       xln[:, xoff:xoff + TB], start=(kc == 0), stop=last).then_inc(sems["gu"], 1)
                              if last else
                              e.matmul(bG[:], wgu[par][:, kc * 128:(kc + 1) * 128],
                                       xln[:, xoff:xoff + TB], start=(kc == 0), stop=last)))
                    guv2 = None
                    for kc in range(16):
                        last = kc == 15
                        if last:
                            guv2 = p.inc(sems["gu"])
                        xoff = ts * HC * TB + kc * TB
                        p.on("tensor", lambda e, par=par, kc=kc, xoff=xoff, bU=bU, last=last:
                             (e.matmul(bU[:], wgu[par][:, 2048 + kc * 128:2048 + (kc + 1) * 128],
                                       xln[:, xoff:xoff + TB], start=(kc == 0), stop=last).then_inc(sems["gu"], 1)
                              if last else
                              e.matmul(bU[:], wgu[par][:, 2048 + kc * 128:2048 + (kc + 1) * 128],
                                       xln[:, xoff:xoff + TB], start=(kc == 0), stop=last)))
                    p.wait("scalar", sems["gu"], guv2)
                    slv = p.inc(sems["sl"])
                    p.on("scalar", lambda e, par2=par2, bG=bG: e.activation(
                        sil[par2][:], bG[:], AF.Silu).then_inc(sems["sl"], 1))
                    p.wait("vector", sems["sl"], slv)
                    epv = p.inc(sems["ep"])
                    ioff = j * MTB + ts * TB
                    if shared:
                        p.on("vector", lambda e, par2=par2, bU=bU, ioff=ioff: e.tensor_tensor(
                            inter[:, ioff:ioff + TB], sil[par2][:], bU[:], ALU.mult).then_inc(sems["ep"], 1))
                    else:
                        p.on("vector", lambda e, par2=par2, bU=bU, ioff=ioff: e.tensor_tensor(
                            sil[par2][:], sil[par2][:], bU[:], ALU.mult))
                        p.on("vector", lambda e, par2=par2, bpar=bpar, ts=ts, ioff=ioff: e.tensor_tensor(
                            inter[:, ioff:ioff + TB], sil[par2][:],
                            bce[bpar][:, ts * TB:(ts + 1) * TB], ALU.mult).then_inc(sems["ep"], 1))
            ep_all = p.val(sems["ep"])
            if int_d is not None and tb == 0 and e_i == 0:
                p.wait("gpsimd", sems["ep"], ep_all)
                p.dma("gpsimd", int_d, inter[:], sems["cd"])
                dbgv = p.dma("gpsimd", bce_d, bce[0][:], sems["cd"])
                p.wait("vector", sems["cd"], dbgv)
                p.wait("gpsimd", sems["cd"], dbgv)
            p.wait("gpsimd", sems["dm"], p.val(sems["dm"]))
            for fc in range(8):
                src = shd_src if shared else dn_src[e_i]
                p.dma("gpsimd", wdn[:, fc * 2048:(fc + 1) * 2048], src[:, fc, :], sems["wd"])
            wdv = p.val(sems["wd"])
            p.wait("tensor", sems["wd"], wdv)
            p.wait("tensor", sems["ep"], ep_all)
            for pas in range(8):
                hq, ts = pas // 2, pas % 2
                gpas = p.val(sems["dm"])
                if gpas >= 1:
                    p.wait("tensor", sems["ev"], p.val(sems["ev"]))
                dmv = None
                for j in range(8):
                    for h4 in range(4):
                        hc = hq * 4 + h4
                        last = j == 7 and h4 == 3
                        if last:
                            dmv = p.inc(sems["dm"])
                        p.on("tensor", lambda e, j=j, h4=h4, hc=hc, ts=ts, last=last:
                             (e.matmul(banks[4 + h4][:], wdn[:, j * 2048 + hc * 128:j * 2048 + (hc + 1) * 128],
                                       inter[:, j * MTB + ts * TB:j * MTB + (ts + 1) * TB],
                                       start=(j == 0), stop=(j == 7)).then_inc(sems["dm"], 1)
                              if last else
                              e.matmul(banks[4 + h4][:], wdn[:, j * 2048 + hc * 128:j * 2048 + (hc + 1) * 128],
                                       inter[:, j * MTB + ts * TB:j * MTB + (ts + 1) * TB],
                                       start=(j == 0), stop=(j == 7))))
                p.wait("vector", sems["dm"], dmv)
                for h4 in range(4):
                    hc = hq * 4 + h4
                    xoff = ts * HC * TB + hc * TB
                    last = h4 == 3
                    if last:
                        p.inc(sems["ev"])
                    p.on("vector", lambda e, h4=h4, xoff=xoff, last=last:
                         (e.scalar_tensor_tensor(xres[:, xoff:xoff + TB], banks[4 + h4][:], 0.5,
                                                 xres[:, xoff:xoff + TB], ALU.mult, ALU.add).then_inc(sems["ev"], 1)
                          if last else
                          e.scalar_tensor_tensor(xres[:, xoff:xoff + TB], banks[4 + h4][:], 0.5,
                                                 xres[:, xoff:xoff + TB], ALU.mult, ALU.add)))

        p.wait("gpsimd", sems["ev"], p.val(sems["ev"]))
        for ts in range(2):
            p.dma("gpsimd", xout_t[:, :, t0 + ts * TB:t0 + (ts + 1) * TB],
                  xres[:, ts * HC * TB:(ts + 1) * HC * TB].rearrange("p (hc t) -> p hc t", t=TB),
                  sems["st"])
    p.wait("gpsimd", sems["st"], p.val(sems["st"]))
    return sems


def build_moe_nc(dbg=False):
    nc = bass.Bass("TRN2", target_bir_lowering=False, debug=False)
    xin = nc.dram_tensor("xin", [D, TC], F32, kind="ExternalInput").ap()
    gate_w = nc.dram_tensor("gate_w", [D, 16], F32, kind="ExternalInput").ap()
    ebias = nc.dram_tensor("ebias", [1, 16], F32, kind="ExternalInput").ap()
    e_gu = [nc.dram_tensor(f"e_gu{e}", [D, D], F32, kind="ExternalInput").ap() for e in range(16)]
    e_dn = [nc.dram_tensor(f"e_dn{e}", [1024, D], F32, kind="ExternalInput").ap() for e in range(16)]
    sh_g = nc.dram_tensor("sh_g", [D, 1024], F32, kind="ExternalInput").ap()
    sh_u = nc.dram_tensor("sh_u", [D, 1024], F32, kind="ExternalInput").ap()
    sh_d = nc.dram_tensor("sh_d", [1024, D], F32, kind="ExternalInput").ap()
    comb_d = nc.dram_tensor("comb_d", [16, TC], F32,
                            kind="ExternalOutput" if dbg else "Internal").ap()
    scod_d = nc.dram_tensor("scod_d", [128, 16 * 16], F32, kind="ExternalOutput").ap() if dbg else None
    xln_d = nc.dram_tensor("xln_d", [128, 2 * HC * TB], BF16, kind="ExternalOutput").ap() if dbg else None
    int_d = nc.dram_tensor("int_d", [128, 8 * MTB], BF16, kind="ExternalOutput").ap() if dbg else None
    bce_d = nc.dram_tensor("bce_d", [128, MTB], F32, kind="ExternalOutput").ap() if dbg else None
    xout = nc.dram_tensor("xout", [D, TC], F32, kind="ExternalOutput").ap()
    with ExitStack() as ctx:
        p = Prog(nc)
        build_moe_program(p, nc, ctx, xin, gate_w, ebias, e_gu, e_dn, sh_g, sh_u, sh_d,
                          comb_d, xout, scod_d=scod_d, xln_d=xln_d, int_d=int_d, bce_d=bce_d)
        block = ctx.enter_context(nc.Block())
        p.play(block)
    return nc


def build_qkv1_program(p, nc, ctx, xin, wqkT, wvT, q_fm, k_fm, v_tm, tag="c"):
    sb = lambda name, shape, dt: ctx.enter_context(nc.sbuf_tensor(tag + name, shape, dt))
    ps = lambda name, shape, dt: ctx.enter_context(nc.psum_tensor(tag + name, shape, dt))
    sem = lambda name: ctx.enter_context(nc.semaphore(tag + name))

    xres = sb("xres", [128, HC * TB], F32)
    xln = sb("xln", [128, HC * TB], BF16)
    xsq = [sb(f"xsq{i}", [128, TB], F32) for i in range(4)]
    stat = sb("stat", [1, 4 * TB], F32)
    bc = [sb(f"bc{i}", [128, TB], F32) for i in range(2)]
    ones_f = sb("ones_f", [128, 1], F32)
    ones_r = sb("ones_r", [1, 128], F32)
    qkbuf = sb("qkbuf", [128, 20 * TB], BF16)
    vobuf = sb("vobuf", [128, 4 * 512], BF16)
    wqk = [sb(f"wqk{i}", [128, 16 * 128], BF16) for i in range(2)]
    wv = sb("wv", [128, 16 * 512], BF16)
    banks = [ps(f"pb{i}", [128, TB], F32) for i in range(8)]

    sems = {k: sem("_" + k) for k in
            ("x", "w", "wv", "sq", "ss", "pe", "dve", "act", "qk", "qe", "vv",
             "ve", "st", "init")}

    p.on("gpsimd", lambda e: e.memset(ones_f[:], 1.0))
    ii = p.inc(sems["init"])
    p.on("gpsimd", lambda e: e.memset(ones_r[:], 1.0).then_inc(sems["init"], 1))
    p.wait("tensor", sems["init"], ii)

    xin_t = xin.rearrange("(hc p) t -> p hc t", p=128)
    wat = wqkT.rearrange("(kc p) m -> p kc m", p=128)
    wvr = wvT.rearrange("(kc p) m -> p kc m", p=128)
    qf = q_fm.rearrange("(mc p) t -> p mc t", p=128)
    kf = k_fm.rearrange("(mc p) t -> p mc t", p=128)
    vt = v_tm.rearrange("(tc p) v -> p tc v", p=128)

    wvv = p.dma("gpsimd", wv[:].rearrange("p (kc v) -> p kc v", v=512), wvr, sems["wv"])

    for tb in range(NTB):
        t0 = tb * TB
        if tb > 0:
            p.wait("gpsimd", sems["st"], 48 * tb)
        xv = p.dma("gpsimd", xres[:].rearrange("p (hc t) -> p hc t", t=TB),
                   xin_t[:, :, t0:t0 + TB], sems["x"])
        d_ln = emit_ln(p, sems, TB, xres[:], xln[:], xsq, ones_f[:], ones_r[:],
                       stat[:], bc, banks, (sems["x"], xv))

        for mc in range(20):
            par = mc % 2
            gmc = p.val(sems["qk"])
            if gmc >= 2:
                p.wait("gpsimd", sems["qk"], gmc - 1)
            wvq = p.dma("gpsimd", wqk[par][:].rearrange("p (kc m) -> p kc m", m=128),
                        wat[:, :, mc * 128:(mc + 1) * 128], sems["w"])
            p.wait("tensor", sems["w"], wvq)
            if mc == 0:
                p.wait("tensor", sems["dve"], d_ln)
            if gmc >= 2:
                p.wait("tensor", sems["qe"], gmc - 1)
            qv = None
            for kc in range(16):
                last = kc == 15
                if last:
                    qv = p.inc(sems["qk"])
                p.on("tensor", lambda e, par=par, kc=kc, last=last:
                     (e.matmul(banks[par][:], wqk[par][:, kc * 128:(kc + 1) * 128],
                               xln[:, kc * TB:(kc + 1) * TB],
                               start=(kc == 0), stop=last).then_inc(sems["qk"], 1)
                      if last else
                      e.matmul(banks[par][:], wqk[par][:, kc * 128:(kc + 1) * 128],
                               xln[:, kc * TB:(kc + 1) * TB], start=(kc == 0), stop=last)))
            p.wait("scalar", sems["qk"], qv)
            qev = p.inc(sems["qe"])
            p.on("scalar", lambda e, mc=mc, par=par: e.copy(
                qkbuf[:, mc * TB:(mc + 1) * TB], banks[par][:]).then_inc(sems["qe"], 1))

        p.wait("tensor", sems["wv"], wvv)
        for tc in range(4):
            gve = p.val(sems["ve"])
            if gve >= 4:
                p.wait("tensor", sems["ve"], gve - 3 + tc)
            vvv = None
            for kc in range(16):
                last = kc == 15
                if last:
                    vvv = p.inc(sems["vv"])
                p.on("tensor", lambda e, tc=tc, kc=kc, last=last:
                     (e.matmul(banks[4 + tc][:], xln[:, kc * TB + tc * 128:kc * TB + (tc + 1) * 128],
                               wv[:, kc * 512:(kc + 1) * 512],
                               start=(kc == 0), stop=last).then_inc(sems["vv"], 1)
                      if last else
                      e.matmul(banks[4 + tc][:], xln[:, kc * TB + tc * 128:kc * TB + (tc + 1) * 128],
                               wv[:, kc * 512:(kc + 1) * 512], start=(kc == 0), stop=last)))
            p.wait("scalar", sems["vv"], vvv)
            vev = p.inc(sems["ve"])
            p.on("scalar", lambda e, tc=tc: e.copy(
                vobuf[:, tc * 512:(tc + 1) * 512], banks[4 + tc][:]).then_inc(sems["ve"], 1))

        p.wait("gpsimd", sems["qe"], p.val(sems["qe"]))
        p.dma("gpsimd", qf[:, :, t0:t0 + TB],
              qkbuf[:, 0:16 * TB].rearrange("p (mc t) -> p mc t", t=TB), sems["st"])
        p.dma("gpsimd", kf[:, :, t0:t0 + TB],
              qkbuf[:, 16 * TB:20 * TB].rearrange("p (mc t) -> p mc t", t=TB), sems["st"])
        p.wait("gpsimd", sems["ve"], p.val(sems["ve"]))
        p.dma("gpsimd", vt[:, 4 * tb:4 * (tb + 1), :],
              vobuf[:].rearrange("p (tc v) -> p tc v", v=512), sems["st"])
    p.wait("gpsimd", sems["st"], p.val(sems["st"]))
    return sems


def build_qkv1_nc():
    nc = bass.Bass("TRN2", target_bir_lowering=False, debug=False)
    xin = nc.dram_tensor("xin", [D, TC], F32, kind="ExternalInput").ap()
    wqkT = nc.dram_tensor("wqkT", [D, 2560], F32, kind="ExternalInput").ap()
    wvT = nc.dram_tensor("wvT", [D, 512], F32, kind="ExternalInput").ap()
    q_fm = nc.dram_tensor("q_fm", [D, TC], BF16, kind="ExternalOutput").ap()
    k_fm = nc.dram_tensor("k_fm", [512, TC], BF16, kind="ExternalOutput").ap()
    v_tm = nc.dram_tensor("v_tm", [TC, 512], BF16, kind="ExternalOutput").ap()
    with ExitStack() as ctx:
        p = Prog(nc)
        build_qkv1_program(p, nc, ctx, xin, wqkT, wvT, q_fm, k_fm, v_tm)
        block = ctx.enter_context(nc.Block())
        p.play(block)
    return nc


KW1 = 3072

def build_attn1_program(p, nc, ctx, xin, q_fm, k1l, v1l, woT, bo_d, vb_d, xout, tag="d"):
    sb = lambda name, shape, dt: ctx.enter_context(nc.sbuf_tensor(tag + name, shape, dt))
    ps = lambda name, shape, dt: ctx.enter_context(nc.psum_tensor(tag + name, shape, dt))
    sem = lambda name: ctx.enter_context(nc.semaphore(tag + name))

    xres = sb("xres", [128, HC * TB], F32)
    qt = sb("qt", [128, 16 * TB], BF16)
    kbuf = sb("kbuf", [128, 4 * 1536], BF16)
    vbuf = sb("vbuf", [128, 12 * 512], BF16)
    ybuf = sb("ybuf", [128, 16 * TB], BF16)
    osb = sb("osb", [128, TB], F32)
    sacc = sb("sacc", [1, 512], F32)
    pt = [sb(f"pt{i}", [128, TB], BF16) for i in range(2)]
    msk = [sb(f"msk{i}", [128, TB], F32) for i in range(8)]
    wp = [sb(f"wp{i}", [128, 16 * 128], BF16) for i in range(2)]
    vb = sb("vbt", [128, 12], F32)
    bo_s = sb("bo_s", [128, 16], F32)
    ones_b = sb("ones_b", [128, 1], BF16)
    ones_r = sb("ones_r", [1, 128], F32)
    banks = [ps(f"pb{i}", [128, TB], F32) for i in range(8)]

    sems = {k: sem("_" + k) for k in
            ("x", "k", "w", "st", "s1", "msk", "exp", "pv", "sev", "ybc", "yy",
             "pj", "ev", "init")}

    for i in range(4):
        p.on("gpsimd", lambda e, i=i: e.memset(msk[i][:], 0.0))
        p.on("gpsimd", lambda e, i=i: e.affine_select(
            out=msk[i][:], in_=msk[i][:], compare_op=ALU.is_gt, fill=NEG,
            base=i * 128, pattern=[[-1, TB]], channel_multiplier=1))
        p.on("gpsimd", lambda e, i=i: e.memset(msk[4 + i][:], 0.0))
        p.on("gpsimd", lambda e, i=i: e.affine_select(
            out=msk[4 + i][:], in_=msk[4 + i][:], compare_op=ALU.is_ge, fill=NEG,
            base=-i * 128, pattern=[[1, TB]], channel_multiplier=-1))
    p.on("gpsimd", lambda e: e.memset(ones_b[:], 1.0))
    ii = p.inc(sems["init"])
    p.on("gpsimd", lambda e: e.memset(ones_r[:], 1.0).then_inc(sems["init"], 1))
    p.wait("tensor", sems["init"], ii)
    p.wait("scalar", sems["init"], ii)
    p.wait("vector", sems["init"], ii)
    p.dma("gpsimd", vb[:], vb_d, sems["x"])
    bov = p.dma("gpsimd", bo_s[:], bo_d, sems["x"])

    xin_t = xin.rearrange("(hc p) t -> p hc t", p=128)
    xout_t = xout.rearrange("(hc p) t -> p hc t", p=128)
    qf = q_fm.rearrange("(mc p) t -> p mc t", p=128)
    kl = k1l.rearrange("(g p) t -> p g t", p=128)
    vl = v1l.rearrange("(kc p) v -> p kc v", p=128)
    wpj = woT.rearrange("(ic p) o -> p ic o", p=128)

    for qb in range(NTB):
        t0 = qb * TB
        if qb > 0:
            p.wait("gpsimd", sems["st"], 16 * qb)
            p.wait("vector", sems["st"], 16 * qb)
        qv = p.dma("gpsimd", qt[:].rearrange("p (h t) -> p h t", t=TB),
                   qf[:, :, t0:t0 + TB], sems["x"])
        kv = p.dma("gpsimd", kbuf[:].rearrange("p (g t) -> p g t", t=1536),
                   kl[:, :, qb * TB:qb * TB + 1536], sems["k"])
        kv = p.dma("gpsimd", vbuf[:].rearrange("p (kc v) -> p kc v", v=512),
                   vl[:, qb * 4:qb * 4 + 12, :], sems["k"])
        p.wait("tensor", sems["k"], kv)
        p.wait("tensor", sems["x"], qv)

        for g in range(4):
            for qh in range(4):
                h = g * 4 + qh
                for s in range(3):
                    for kc in range(4):
                        ch = s * 4 + kc
                        spar = ch % 2
                        gkc = p.val(sems["s1"])
                        if gkc >= 2:
                            p.wait("tensor", sems["exp"], gkc - 1)
                        sv = p.inc(sems["s1"])
                        p.on("tensor", lambda e, spar=spar, g=g, s=s, kc=kc, h=h:
                             e.matmul(banks[spar][:], kbuf[:, g * 1536 + s * 512 + kc * 128:g * 1536 + s * 512 + (kc + 1) * 128],
                                      qt[:, h * TB:(h + 1) * TB], start=True, stop=True).then_inc(sems["s1"], 1))
                        if s == 0 or s == 2:
                            p.wait("vector", sems["s1"], sv)
                            mv = p.inc(sems["msk"])
                            mi = kc if s == 0 else 4 + kc
                            p.on("vector", lambda e, spar=spar, mi=mi: e.tensor_tensor(
                                banks[spar][:], banks[spar][:], msk[mi][:], ALU.add).then_inc(sems["msk"], 1))
                            p.wait("scalar", sems["msk"], mv)
                        else:
                            p.wait("scalar", sems["s1"], sv)
                        gpt = p.val(sems["exp"])
                        if gpt >= 2:
                            p.wait("scalar", sems["pv"], gpt - 1)
                        ppar = gpt % 2
                        jcol = qb * 3 + s
                        xv2 = p.inc(sems["exp"])
                        p.on("scalar", lambda e, spar=spar, ppar=ppar, jcol=jcol: e.activation(
                            pt[ppar][:], banks[spar][:], AF.Exp, bias=vb[:, jcol:jcol + 1],
                            scale=ISQ).then_inc(sems["exp"], 1))
                        p.wait("tensor", sems["exp"], xv2)
                        p.on("tensor", lambda e, ppar=ppar, ch=ch: e.matmul(
                            banks[3][0:1, :], ones_b[:], pt[ppar][:],
                            start=(ch == 0), stop=(ch == 11)))
                        pvv = p.inc(sems["pv"])
                        p.on("tensor", lambda e, ppar=ppar, ch=ch, g=g: e.matmul(
                            banks[2][:], vbuf[:, ch * 512 + g * 128:ch * 512 + (g + 1) * 128],
                            pt[ppar][:], start=(ch == 0), stop=(ch == 11)).then_inc(sems["pv"], 1))
                p.wait("vector", sems["pv"], p.val(sems["pv"]))
                p.wait("scalar", sems["pv"], p.val(sems["pv"]))
                oev = p.inc(sems["sev"])
                p.on("scalar", lambda e: e.copy(osb[:], banks[2][:]).then_inc(sems["sev"], 1))
                dv = p.inc(sems["sev"])
                p.on("vector", lambda e: e.tensor_copy(sacc[:], banks[3][0:1, :]).then_inc(sems["sev"], 1))
                p.wait("vector", sems["sev"], dv)
                dv2 = p.inc(sems["sev"])
                p.on("vector", lambda e: e.reciprocal(sacc[:], sacc[:]).then_inc(sems["sev"], 1))
                p.wait("tensor", sems["sev"], dv2)
                bv = p.inc(sems["ybc"])
                p.on("tensor", lambda e: e.matmul(
                    banks[4][:], ones_r[:], sacc[:], start=True, stop=True).then_inc(sems["ybc"], 1))
                p.wait("vector", sems["ybc"], bv)
                p.wait("vector", sems["sev"], oev)
                yv = p.inc(sems["yy"])
                p.on("vector", lambda e, h=h: e.tensor_tensor(
                    ybuf[:, h * TB:(h + 1) * TB], osb[:], banks[4][:],
                    ALU.mult).then_inc(sems["yy"], 1))
                p.wait("tensor", sems["yy"], p.val(sems["yy"]))
                p.wait("scalar", sems["yy"], p.val(sems["yy"]))

        p.wait("gpsimd", sems["yy"], p.val(sems["yy"]))
        xv = p.dma("gpsimd", xres[:].rearrange("p (hc t) -> p hc t", t=TB),
                   xin_t[:, :, t0:t0 + TB], sems["x"])
        y_all = p.val(sems["yy"])
        for oc in range(HC):
            par = oc % 2
            goc = p.val(sems["pj"])
            if goc >= 2:
                p.wait("gpsimd", sems["pj"], goc - 1)
            wv2 = p.dma("gpsimd", wp[par][:].rearrange("p (ic o) -> p ic o", o=128),
                        wpj[:, :, oc * 128:(oc + 1) * 128], sems["w"])
            p.wait("tensor", sems["w"], wv2)
            if oc == 0:
                p.wait("tensor", sems["yy"], y_all)
            if goc >= 2:
                p.wait("tensor", sems["ev"], goc - 1)
            pjv = None
            for ic in range(HC):
                last = ic == HC - 1
                if last:
                    pjv = p.inc(sems["pj"])
                p.on("tensor", lambda e, par=par, ic=ic, oc=oc, last=last:
                     (e.matmul(banks[5 + oc % 2][:], wp[par][:, ic * 128:(ic + 1) * 128],
                               ybuf[:, ic * TB:(ic + 1) * TB],
                               start=(ic == 0), stop=last).then_inc(sems["pj"], 1)
                      if last else
                      e.matmul(banks[5 + oc % 2][:], wp[par][:, ic * 128:(ic + 1) * 128],
                               ybuf[:, ic * TB:(ic + 1) * TB], start=(ic == 0), stop=last)))
            p.wait("vector", sems["pj"], pjv)
            if oc == 0:
                p.wait("vector", sems["x"], xv)
                p.wait("vector", sems["x"], bov)
            evv = p.inc(sems["ev"])
            p.on("vector", lambda e, oc=oc: e.scalar_tensor_tensor(
                xres[:, oc * TB:(oc + 1) * TB], banks[5 + oc % 2][:], bo_s[:, oc:oc + 1],
                xres[:, oc * TB:(oc + 1) * TB], ALU.add, ALU.add).then_inc(sems["ev"], 1))
        p.wait("gpsimd", sems["ev"], p.val(sems["ev"]))
        p.dma("gpsimd", xout_t[:, :, t0:t0 + TB],
              xres[:].rearrange("p (hc t) -> p hc t", t=TB), sems["st"])
    p.wait("gpsimd", sems["st"], p.val(sems["st"]))
    return sems


def build_attn1_nc():
    nc = bass.Bass("TRN2", target_bir_lowering=False, debug=False)
    xin = nc.dram_tensor("xin", [D, TC], F32, kind="ExternalInput").ap()
    q_fm = nc.dram_tensor("q_fm", [D, TC], BF16, kind="ExternalInput").ap()
    k1l = nc.dram_tensor("k1l", [512, KW1], BF16, kind="ExternalInput").ap()
    v1l = nc.dram_tensor("v1l", [KW1, 512], BF16, kind="ExternalInput").ap()
    woT = nc.dram_tensor("woT", [D, D], F32, kind="ExternalInput").ap()
    bo_d = nc.dram_tensor("bo_d", [128, HC], F32, kind="ExternalInput").ap()
    vb_d = nc.dram_tensor("vb_d", [128, 12], F32, kind="ExternalInput").ap()
    xout = nc.dram_tensor("xout", [D, TC], F32, kind="ExternalOutput").ap()
    with ExitStack() as ctx:
        p = Prog(nc)
        build_attn1_program(p, nc, ctx, xin, q_fm, k1l, v1l, woT, bo_d, vb_d, xout)
        block = ctx.enter_context(nc.Block())
        p.play(block)
    return nc


def attn1_host_kv(k_shards, v_shards, c):
    cp = c % 4
    grp = range((c // 4) * 4, (c // 4) * 4 + 4)
    k_seq = np.concatenate([k_shards[g] for g in grp], axis=1)
    v_seq = np.concatenate([v_shards[g] for g in grp], axis=0)
    k1l = np.zeros((512, KW1), k_seq.dtype)
    v1l = np.zeros((KW1, 512), v_seq.dtype)
    lo = cp * TC - 1024
    src_lo = max(lo, 0)
    off = src_lo - lo
    k1l[:, off:] = k_seq[:, src_lo:cp * TC + TC]
    v1l[off:, :] = v_seq[src_lo:cp * TC + TC, :]
    return np.ascontiguousarray(k1l), np.ascontiguousarray(v1l)


def attn1_host_vb(c):
    cp = c % 4
    vb = np.zeros((128, 12), np.float32)
    if cp == 0:
        for qb in range(4):
            for s in range(3):
                if qb * TB + s * TB < 1024:
                    vb[:, qb * 3 + s] = NEG
    return vb


def kernel(**inputs):
    pre = _prep_shared(inputs)
    cores = list(range(NCORES))
    x = inputs["L0_attn_x"]

    nc_q = build_dequant_nc()
    r = run_bass_kernel_spmd(nc_q, [dequant_shard_inputs(pre, c) for c in cores], core_ids=cores)
    wg_lo, wg_hi, wu_lo, wu_hi, wd_lo, wd_hi = dequant_assemble(r.results)
    wffn = dict(wg_lo=wg_lo, wg_hi=wg_hi, wu_lo=wu_lo, wu_hi=wu_hi, wd_lo=wd_lo, wd_hi=wd_hi)

    nc_f = build_ffn_nc()
    xs = [_x_shard(x, c) for c in cores]
    r = run_bass_kernel_spmd(nc_f, [dict(xin=xs[c], **wffn) for c in cores], core_ids=cores)
    xs = [r.results[c]["xout"] for c in cores]

    nc_qkv = build_qkv0_nc()
    maps = []
    for c in cores:
        cos, sin = _rope_tables((c % 4) * TC)
        maps.append(dict(xin=xs[c], watT=pre["w_attenT"], cosd=cos, sind=sin))
    r = run_bass_kernel_spmd(nc_qkv, maps, core_ids=cores)
    kk = [r.results[c]["k_fm"] for c in cores]
    vv = [r.results[c]["v_tm"] for c in cores]
    qq = [r.results[c]["q_fm"] for c in cores]
    nc_a0 = build_attn0_nc()
    maps = []
    for c in cores:
        k_loc, v_loc = attn0_host_kv(kk, vv, c)
        maps.append(dict(xin=xs[c], q_fm=qq[c], k_loc=k_loc, v_loc=v_loc,
                         wprojT=pre["w_projT"], vb_d=attn0_host_vb(c)))
    r = run_bass_kernel_spmd(nc_a0, maps, core_ids=cores)
    xs = [r.results[c]["xout"] for c in cores]

    r = run_bass_kernel_spmd(nc_f, [dict(xin=xs[c], **wffn) for c in cores], core_ids=cores)
    xs = [r.results[c]["xout"] for c in cores]

    nc_m = build_moe_nc()
    def moe_maps():
        maps = []
        for c in cores:
            m = dict(xin=xs[c], gate_w=pre["gate_w"], ebias=pre["ebias"],
                     sh_g=pre["sh_g"], sh_u=pre["sh_u"], sh_d=pre["sh_d"])
            for e in range(16):
                m[f"e_gu{e}"] = pre["e_gu"][e]
                m[f"e_dn{e}"] = pre["e_dn"][e]
            maps.append(m)
        return maps
    r = run_bass_kernel_spmd(nc_m, moe_maps(), core_ids=cores)
    xs = [r.results[c]["xout"] for c in cores]

    nc_q1 = build_qkv1_nc()
    wqkT = np.ascontiguousarray(np.concatenate([pre["WqT"], pre["WkT"]], axis=1))
    r = run_bass_kernel_spmd(nc_q1, [dict(xin=xs[c], wqkT=wqkT, wvT=pre["WvT"]) for c in cores],
                             core_ids=cores)
    kk = [r.results[c]["k_fm"] for c in cores]
    vv = [r.results[c]["v_tm"] for c in cores]
    qq = [r.results[c]["q_fm"] for c in cores]
    nc_a1 = build_attn1_nc()
    maps = []
    for c in cores:
        k1l, v1l = attn1_host_kv(kk, vv, c)
        maps.append(dict(xin=xs[c], q_fm=qq[c], k1l=k1l, v1l=v1l,
                         woT=pre["WoT"], bo_d=pre["bo"], vb_d=attn1_host_vb(c)))
    r = run_bass_kernel_spmd(nc_a1, maps, core_ids=cores)
    xs = [r.results[c]["xout"] for c in cores]

    r = run_bass_kernel_spmd(nc_m, moe_maps(), core_ids=cores)
    xs = [r.results[c]["xout"] for c in cores]

    return _x_unshard(xs)
